# version 06
# baseline (speedup 1.0000x reference)
from contextlib import ExitStack

import numpy as np
import ml_dtypes

import concourse.bass as bass
import concourse.mybir as mybir
from concourse.bass import IndirectOffsetOnAxis
from concourse.bass_utils import run_bass_kernel_spmd

F32 = mybir.dt.float32
BF16 = mybir.dt.bfloat16
I32 = mybir.dt.int32
U32 = mybir.dt.uint32
AF = mybir.ActivationFunctionType
ALU = mybir.AluOpType
AX = mybir.AxisListType.X

NCORE = 8
T = 2048
D = 1024
CH = 2048
DK = 8
CTN = 16
TT = 16
NE = 32
CL = 384
CAP = 2560
DST = 16
SLOT_ZERO = NE * CL
SLOT_TRASH = NE * CL + 1
NSLOT = NE * CL + 2
BIG = 1.0e30

bf = ml_dtypes.bfloat16
DEBUG_TAPS = False


def _tile_kmaj(w):
    K, M = w.shape
    return np.ascontiguousarray(
        w.reshape(K // 128, 128, M).transpose(1, 0, 2).reshape(128, (K // 128) * M))


def host_prep(inp):
    x = np.ascontiguousarray(np.asarray(inp["L0_attn_x"], np.float32).reshape(16384, D))
    w_in = np.asarray(inp["L0_attn_w_in"], np.float32)
    b_in = np.asarray(inp["L0_attn_b_in"], np.float32)
    w_conv = np.asarray(inp["L0_attn_w_conv"], np.float32)
    b_conv = np.asarray(inp["L0_attn_b_conv"], np.float32)
    w_xproj = np.asarray(inp["L0_attn_w_xproj"], np.float32)
    w_dt = np.asarray(inp["L0_attn_w_dt"], np.float32)
    b_dt = np.asarray(inp["L0_attn_b_dt"], np.float32)
    w_out = np.asarray(inp["L0_attn_w_out"], np.float32)
    wg = np.asarray(inp["L0_ffn_wg"], np.float32)
    w1 = np.asarray(inp["L0_ffn_w1"], np.float32)
    w2 = np.asarray(inp["L0_ffn_w2"], np.float32)
    w3 = np.asarray(inp["L0_ffn_w3"], np.float32)
    sw1 = np.asarray(inp["L0_ffn_sw1"], np.float32)
    sw2 = np.asarray(inp["L0_ffn_sw2"], np.float32)
    sw3 = np.asarray(inp["L0_ffn_sw3"], np.float32)

    sh = {}
    w_inT = np.ascontiguousarray(w_in.T)
    sh["w_inT_t"] = _tile_kmaj(np.concatenate([w_inT[:, :CH], w_inT[:, CH:CH + 16]], 1))
    W80 = np.zeros((80, CH), np.float32)
    W80[0:16] = w_xproj[0:16]
    W80[32:48] = w_dt
    W80[64:80] = w_xproj[16:32]
    sh["wxdtT_t"] = _tile_kmaj(np.ascontiguousarray(W80.T))
    sh["wgT_t"] = _tile_kmaj(np.ascontiguousarray(wg.T))
    sh["w_out16T"] = np.ascontiguousarray(w_out[:, :DST].T)
    sh["sw1_t"] = _tile_kmaj(sw1)
    sh["sw3_t"] = _tile_kmaj(sw3)
    sh["sw2_t"] = _tile_kmaj(sw2)
    w1r = w1.reshape(NE, 8, 128, 2, 512).transpose(0, 3, 2, 1, 4)
    w3r = w3.reshape(NE, 8, 128, 2, 512).transpose(0, 3, 2, 1, 4)
    sh["w13_t"] = np.ascontiguousarray(
        np.concatenate([w1r, w3r], axis=4).reshape(NE, 2, 128, 8192))
    sh["w2_t"] = np.ascontiguousarray(
        w2.reshape(NE, 8, 128, 1024).transpose(0, 2, 1, 3).reshape(NE, 128, 8192))
    sh["b_in_xc"] = np.ascontiguousarray(b_in[:CH].reshape(CTN, 128).T)
    sh["b_in_z"] = np.ascontiguousarray(b_in[CH:CH + 16].reshape(16, 1))
    sh["b_conv_t"] = np.ascontiguousarray(b_conv.reshape(CTN, 128).T)
    sh["w_conv_t"] = np.ascontiguousarray(
        w_conv[:, 0, :].reshape(CTN, 128, 4).transpose(1, 0, 2).reshape(128, CTN * 4))
    sh["b_dt"] = np.ascontiguousarray(b_dt.reshape(16, 1))
    q = np.arange(128)
    sh["t_incl"] = (q[:, None] <= q[None, :]).astype(bf)
    sh["ones_full"] = np.ones((128, 128), bf)
    sh["iden"] = np.eye(128, dtype=bf)
    sh["iota32"] = np.tile(np.arange(32, dtype=np.float32), (128, 1))
    sh["ones_col"] = np.ones((128, 1), bf)

    cores = []
    for c in range(NCORE):
        pc = {}
        pc["x_sh"] = np.ascontiguousarray(x[c * T:(c + 1) * T])
        if c == 0:
            pc["x_halo"] = np.zeros((3, D), np.float32)
            pc["halo_flag"] = np.zeros((128, 1), np.float32)
        else:
            pc["x_halo"] = np.ascontiguousarray(x[c * T - 3:c * T])
            pc["halo_flag"] = np.ones((128, 1), np.float32)
        m8 = (np.arange(8) < c).astype(np.float32)
        pc["strict8_rep"] = np.ascontiguousarray(np.tile(m8[:, None], (1, 128))).astype(bf)
        pc["mask16"] = np.ascontiguousarray(np.tile(m8[None, :], (16, 1)))
        pc["invmask16"] = np.ascontiguousarray(np.tile((1.0 - m8)[None, :], (16, 1)))
        cores.append(pc)
    return sh, cores


SHARED_SPECS = [
    ("w_inT_t", [128, DK * (CH + 16)], F32),
    ("wxdtT_t", [128, CTN * 80], F32),
    ("wgT_t", [128, DK * 32], F32),
    ("w_out16T", [16, 1024], F32),
    ("sw1_t", [128, 8192], F32),
    ("sw3_t", [128, 8192], F32),
    ("sw2_t", [128, 8192], F32),
    ("w13_t", [NE, 2, 128, 8192], F32),
    ("w2_t", [NE, 128, 8192], F32),
    ("b_in_xc", [128, CTN], F32),
    ("b_in_z", [16, 1], F32),
    ("b_conv_t", [128, CTN], F32),
    ("w_conv_t", [128, CTN * 4], F32),
    ("b_dt", [16, 1], F32),
    ("t_incl", [128, 128], BF16),
    ("ones_full", [128, 128], BF16),
    ("iden", [128, 128], BF16),
    ("iota32", [128, 32], F32),
    ("ones_col", [128, 1], BF16),
]
CORE_SPECS = [
    ("x_sh", [T, D], F32),
    ("x_halo", [3, D], F32),
    ("halo_flag", [128, 1], F32),
    ("strict8_rep", [8, 128], BF16),
    ("mask16", [16, 8], F32),
    ("invmask16", [16, 8], F32),
]


EH_W1 = [1, 3, 5, 7, 8, 10, 12, 14]
EH_W3 = [2, 4, 6, None, 9, 11, 13, None]


class Ch:

    def __init__(self, q, s):
        self.q = q
        self.s = s
        self.n = 0

    def do(self, inst):
        inst.then_inc(self.s, 1)
        self.n += 1
        self.q.wait_ge(self.s, self.n)
        return inst


def build_nc():
    nc = bass.Bass("TRN2", target_bir_lowering=False)
    IN = {}
    for name, shape, dt in SHARED_SPECS + CORE_SPECS:
        IN[name] = nc.dram_tensor(name, shape, dt, kind="ExternalInput")
    y_out = nc.dram_tensor("y_out", [T, D], F32, kind="ExternalOutput")
    tap_specs = [
        ("tp_lg", [128, TT * 32], F32), ("tp_ea", [128, TT * 32], F32),
        ("tp_idx", [128, TT * 4], F32), ("tp_wn", [128, TT * 4], F32),
        ("tp_wt", [128, TT * 4], F32), ("tp_lp", [128, TT * 4], F32),
        ("tp_gp", [128, TT * 4], F32), ("tp_sd", [128, TT * 4], I32),
        ("tp_gd", [128, TT * 4], I32), ("tp_cnt", [128, TT * 32], BF16),
        ("tp_cag", [8, 32], F32), ("tp_coff", [128, 32], F32),
        ("tp_hf", [16, T], F32), ("tp_z16", [16, T], BF16),
        ("tp_hz", [16, T], BF16), ("tp_tot", [1, 32], F32),
    ]
    taps = {}
    if DEBUG_TAPS:
        for nm, shp, dt in tap_specs:
            taps[nm] = nc.dram_tensor(nm, shp, dt, kind="ExternalOutput")

    xn_scr = nc.dram_tensor("xn_scr", [T + 1, D], BF16)
    xe_tbl = nc.dram_tensor("xe_tbl", [NSLOT, D], BF16)
    out_tbl = nc.dram_tensor("out_tbl", [NSLOT, D], BF16)
    z_scr = nc.dram_tensor("z_scr", [T, D], F32)
    cc1_in = nc.dram_tensor("cc1_in", [1, 64], F32)
    cc1_out = nc.dram_tensor("cc1_out", [8, 64], F32)
    cc2_in = nc.dram_tensor("cc2_in", [1, 64], F32)
    cc2_out = nc.dram_tensor("cc2_out", [8, 64], F32)

    es = ExitStack()
    with es:
        def sb(scope, name, shape, dt):
            return scope.enter_context(nc.sbuf_tensor(name, shape, dt))

        def ps(scope, name, shape, dt):
            return scope.enter_context(nc.psum_tensor(name, shape, dt))

        def sem(name):
            return es.enter_context(nc.semaphore(name))

        x2 = [sb(es, f"x2_{i}", [128, D], F32) for i in range(4)]
        xnbf = [sb(es, f"xnbf_{i}", [128, D], BF16) for i in range(4)]
        zero_bf = sb(es, "zero_bf", [1, D], BF16)
        c_b_in_xc = sb(es, "c_b_in_xc", [128, CTN], F32)
        c_b_in_z = sb(es, "c_b_in_z", [16, 1], F32)
        c_b_conv = sb(es, "c_b_conv", [128, CTN], F32)
        c_w_conv = sb(es, "c_w_conv", [128, CTN * 4], F32)
        c_b_dt = sb(es, "c_b_dt", [16, 1], F32)
        c_tincl = sb(es, "c_tincl", [128, 128], BF16)
        c_onesf = sb(es, "c_onesf", [128, 128], BF16)
        c_iden = sb(es, "c_iden", [128, 128], BF16)
        c_iota = sb(es, "c_iota", [128, 32], F32)
        c_onesc = sb(es, "c_onesc", [128, 1], BF16)
        c_strict = sb(es, "c_strict", [8, 128], BF16)
        c_m16 = sb(es, "c_m16", [16, 8], F32)
        c_im16 = sb(es, "c_im16", [16, 8], F32)
        c_hflag = sb(es, "c_hflag", [128, 1], F32)
        cm1 = sb(es, "cm1", [128, 1], F32)
        c1 = sb(es, "c1", [128, 1], F32)
        halo_b = sb(es, "halo_b", [128, CTN], F32)
        wout_sb = sb(es, "wout_sb", [16, 1024], BF16)

        M16f = sb(es, "M16f", [16, 4 * T], F32)
        M16b = sb(es, "M16b", [16, 2 * T], BF16)

        def S16(tensor, slot):
            class V:
                def __getitem__(self, key):
                    cols = key[1] if isinstance(key, tuple) else slice(None)
                    start = cols.start if cols.start is not None else 0
                    stop = cols.stop if cols.stop is not None else T
                    return tensor[0:16, slot * T + start:slot * T + stop]
            return V()

        absA = S16(M16f, 0); A_sb = absA; h_f = absA
        sp_r = S16(M16f, 1); h_loc = sp_r
        sp_t = S16(M16f, 2); delta = sp_t; Acum = sp_t
        xdblB = S16(M16f, 3); Bx_sb = xdblB; zeros16 = xdblB
        z16 = S16(M16b, 0); xc16 = S16(M16b, 1)
        hz = sb(es, "hz", [16, T], BF16)
        a_all = sb(es, "a_all", [16, 8], F32)
        b_all = sb(es, "b_all", [16, 8], F32)
        sfx = sb(es, "sfx", [16, 16], F32)
        atl = sb(es, "atl", [16, 8], F32)
        bm_ = sb(es, "bm_", [16, 8], F32)
        h_in = sb(es, "h_in", [16, 1], F32)
        cnt_bf = sb(es, "cnt_bf", [128, TT * 32], BF16)
        idxf = sb(es, "idxf", [128, TT * 4], F32)
        wn_all = sb(es, "wn_all", [128, TT * 4], F32)
        lpos_all = sb(es, "lpos_all", [128, TT * 4], F32)
        gpos_all = sb(es, "gpos_all", [128, TT * 4], F32)
        wt_all = sb(es, "wt_all", [128, TT * 4], F32)
        sdest = sb(es, "sdest", [128, TT * 4], I32)
        gdest = sb(es, "gdest", [128, TT * 4], I32)
        tmpa = sb(es, "tmpa", [128, TT * 4], F32)
        tmpb = sb(es, "tmpb", [128, TT * 4], F32)
        cagf = sb(es, "cagf", [8, 32], F32)
        cag_hi = sb(es, "cag_hi", [8, 32], BF16)
        cag_hif = sb(es, "cag_hif", [8, 32], F32)
        cag_lo = sb(es, "cag_lo", [8, 32], BF16)
        coff = sb(es, "coff", [128, 32], F32)
        tot_f = sb(es, "tot_f", [1, 32], F32)
        lposg = sb(es, "lposg", [128, 32], F32)
        gposg = sb(es, "gposg", [128, 32], F32)
        oh_t = sb(es, "oh_t", [128, 32], F32)
        rtmp = sb(es, "rtmp", [128, 32], F32)
        rt1 = sb(es, "rt1", [128, 1], F32)
        lg_all = sb(es, "lg_all", [128, TT * 32], F32)
        e_all = sb(es, "e_all", [128, TT * 32], F32)
        nmx = sb(es, "nmx", [128, TT], F32)
        gsc = sb(es, "gsc", [128, 8], F32)
        g8 = sb(es, "g8", [128, 8], F32)
        kgB = sb(es, "kgB", [128, 8], F32)
        msk = sb(es, "msk", [128, 32], F32)
        mx4 = sb(es, "mx4", [128, 8], F32)
        mxi = sb(es, "mxi", [128, 8], U32)
        mnv = sb(es, "mnv", [128, 1], F32)
        bnst = sb(es, "bnst", [128, 12], F32)
        mv = sb(es, "mv", [128, 2], F32)
        vtmp = sb(es, "vtmp", [128, 1], F32)
        sqv = sb(es, "sqv", [128, 1], F32)
        rstd = sb(es, "rstd", [128, 1], F32)

        sc_xnfm = ExitStack()
        xn_fm = sb(sc_xnfm, "xn_fm", [128, DK * T], BF16)
        sc_f1 = ExitStack()
        w_inT = sb(sc_f1, "w_inT", [128, DK * (CH + 16)], BF16)
        wxdt_sb = sb(sc_f1, "wxdt_sb", [128, CTN * 80], BF16)
        wg_sb = sb(sc_f1, "wg_sb", [128, DK * 32], BF16)
        xcp2 = [sb(sc_f1, f"xcp2_{i}", [128, T + 3], BF16) for i in range(2)]
        acc2 = [sb(sc_f1, f"acc2_{i}", [128, T], F32) for i in range(2)]
        xc2 = [sb(sc_f1, f"xc2_{i}", [128, T], BF16) for i in range(2)]
        xh = sb(sc_f1, "xh", [3, D], F32)
        xnh_bf = sb(sc_f1, "xnh_bf", [3, D], BF16)
        xnh_fm = sb(sc_f1, "xnh_fm", [128, DK * 3], BF16)
        sc_f1.close()
        sc_sh = ExitStack()
        hsh = sb(sc_sh, "hsh", [128, 8 * T], BF16)
        zt2 = [sb(sc_sh, f"zt2_{i}", [128, 512], F32) for i in range(2)]
        stmp_s = [sb(sc_sh, f"stmp_s{i}", [128, 512], BF16) for i in range(2)]
        sc_s13 = ExitStack()
        sw1_sb = sb(sc_s13, "sw1_sb", [128, 8192], BF16)
        sw3_sb = sb(sc_s13, "sw3_sb", [128, 8192], BF16)
        sc_s13.close()
        sw2_sb = sb(sc_sh, "sw2_sb", [128, 8192], BF16)
        sc_sh.close()
        sc_xnfm.close()
        sc_ex = ExitStack()
        w13h = [sb(sc_ex, f"w13h_{i}", [128, 8192], BF16) for i in range(2)]
        w2b = [sb(sc_ex, f"w2b_{i}", [128, 8192], BF16) for i in range(2)]
        xe2 = [sb(sc_ex, f"xe2_{i}", [128, DK * CL], BF16) for i in range(2)]
        h_e = [sb(sc_ex, f"h_e{i}", [128, 8 * CL], BF16) for i in range(2)]
        stmp_e = [sb(sc_ex, f"stmp_e{i}", [128, CL], BF16) for i in range(2)]
        out_e = [sb(sc_ex, f"out_e{i}", [128, 512], BF16) for i in range(2)]
        sc_ex.close()
        sc_fin = ExitStack()
        y2 = [sb(sc_fin, f"y2_{i}", [128, D], F32) for i in range(2)]
        z2 = [sb(sc_fin, f"z2_{i}", [128, D], F32) for i in range(4)]
        gat = [[sb(sc_fin, f"gat{k}_{i}", [128, D], BF16) for i in range(2)]
               for k in range(4)]
        sc_fin.close()

        pc_a = ExitStack()
        p_gate = [ps(pc_a, f"p_gate{i}", [128, 32], F32) for i in range(2)]
        p_htp = ps(pc_a, "p_htp", [128, 3], BF16)
        pc_a.close()
        pc_b = ExitStack()
        p_hl = ps(pc_b, "p_hl", [128, 3], F32)
        p_xz = [ps(pc_b, f"p_xz{i}", [128, 512], F32) for i in range(2)]
        p_xd = [ps(pc_b, f"p_xd{i}", [80, 512], F32) for i in range(4)]
        pc_b.close()
        pc_zs = ExitStack()
        p_zs = [ps(pc_zs, f"p_zs{i}", [128, 512], F32) for i in range(2)]
        pc_d = ExitStack()
        p_sh1 = [ps(pc_d, f"p_sh1_{i}", [128, 512], F32) for i in range(2)]
        p_sh3 = [ps(pc_d, f"p_sh3_{i}", [128, 512], F32) for i in range(2)]
        pc_d.close()
        pc_c = ExitStack()
        p_tot = ps(pc_c, "p_tot", [1, 32], F32)
        p_coff = ps(pc_c, "p_coff", [128, 32], F32)
        p_pfx = [ps(pc_c, f"p_pfx{i}", [128, 32], F32) for i in range(2)]
        pc_c.close()
        pc_zs.close()
        pc_e = ExitStack()
        p_eh1 = [ps(pc_e, f"p_eh1_{i}", [128, CL], F32) for i in range(2)]
        p_eh3 = [ps(pc_e, f"p_eh3_{i}", [128, CL], F32) for i in range(2)]
        p_eo = [ps(pc_e, f"p_eo{i}", [128, 512], F32) for i in range(2)]
        pc_e.close()
        p_mo = [[ps(es, f"p_mo{p}_{i}", [128, 512], F32) for i in range(2)]
                for p in range(2)]

        VCH = sem("VCH"); ACH = sem("ACH")
        LDC = sem("LDC"); ZT = sem("ZT")
        LDXb = [sem(f"LDX{i}") for i in range(4)]
        SCRb = [sem(f"SCR{i}") for i in range(4)]
        XNT = sem("XNT"); XNFM = sem("XNFM")
        S_VAR = sem("S_VAR"); S_SQ = sem("S_SQ"); S_HALO = sem("S_HALO")
        S_HTP = sem("S_HTP"); S_HFM = sem("S_HFM")
        WFR = sem("WFR"); SWL = sem("SWL")
        GTP = sem("GTP"); LGT = sem("LGT"); RT3 = sem("RT3")
        S_EX1 = sem("S_EX1"); S_EX2 = sem("S_EX2")
        RMM = sem("RMM"); REV = sem("REV")
        CCS = sem("CCS"); CCT = sem("CCT"); CCR = sem("CCR")
        CAGB = sem("CAGB")
        PFX = sem("PFX"); RTB = sem("RTB"); RTBF = sem("RTBF")
        XZH = sem("XZH"); XZP = sem("XZP"); XCE = sem("XCE"); XCEH = sem("XCEH")
        CVD = sem("CVD"); XCS = sem("XCS"); XDM = sem("XDM"); XDE = sem("XDE")
        S_A1 = sem("S_A1"); S_A2 = sem("S_A2")
        S_CAR = sem("S_CAR")
        S_HZ = sem("S_HZ")
        SCTb = [sem(f"SCT{i}") for i in range(4)]
        SHP = sem("SHP"); SHS = sem("SHS"); SHV = sem("SHV")
        ZSP = sem("ZSP"); ZSE = sem("ZSE")
        FD = sem("FD")
        EWA = sem("EWA"); EWB = sem("EWB")
        EW2b = [sem(f"EW2{i}") for i in range(2)]
        XELb = [sem(f"XEL{i}") for i in range(2)]
        F_A = sem("F_A"); F_B = sem("F_B")
        EH = sem("EH"); EHSS = sem("EHSS"); EHV = sem("EHV")
        EO = sem("EO"); EOE = sem("EOE")
        EOSb = [sem(f"EOS{i}") for i in range(2)]
        FLb = [sem(f"FL{i}") for i in range(4)]
        GTHb = [sem(f"GTH{i}") for i in range(2)]
        ZSTb = [sem(f"ZST{i}") for i in range(2)]
        FPM = sem("FPM"); FCV = sem("FCV")
        YSTb = [sem(f"YST{i}") for i in range(2)]

        block = es.enter_context(nc.Block())

        @block.sync
        def _(q):
            for ap_sb, name in [
                (c_b_in_xc, "b_in_xc"), (c_b_in_z, "b_in_z"), (c_b_conv, "b_conv_t"),
                (c_w_conv, "w_conv_t"), (c_b_dt, "b_dt"), (c_tincl, "t_incl"),
                (c_onesf, "ones_full"), (c_iden, "iden"), (c_iota, "iota32"),
                (c_onesc, "ones_col"), (c_strict, "strict8_rep"), (c_m16, "mask16"),
                (c_im16, "invmask16"), (c_hflag, "halo_flag"), (xh, "x_halo"),
            ]:
                q.dma_start(out=ap_sb[:], in_=IN[name].ap()).then_inc(LDC, 16)
            q.wait_ge(ZT, 1)
            q.dma_start(out=xn_scr.ap()[T:T + 1, :], in_=zero_bf[:]).then_inc(LDC, 16)
            q.dma_start(out=out_tbl.ap()[SLOT_ZERO:SLOT_ZERO + 1, :], in_=zero_bf[:]
                        ).then_inc(LDC, 16)
            for t in range(TT):
                if t >= 4:
                    q.wait_ge(XNT, t - 3)
                q.dma_start(out=x2[t % 4][:],
                            in_=IN["x_sh"].ap()[t * 128:(t + 1) * 128, :]
                            ).then_inc(LDXb[t % 4], 16)
                if t >= 1:
                    q.wait_ge(XNT, t)
                    q.dma_start(out=xn_scr.ap()[(t - 1) * 128:t * 128, :],
                                in_=xnbf[(t - 1) % 4][:]).then_inc(SCRb[(t - 1) % 4], 16)
            q.wait_ge(XNT, TT)
            q.dma_start(out=xn_scr.ap()[(TT - 1) * 128:TT * 128, :],
                        in_=xnbf[(TT - 1) % 4][:]).then_inc(SCRb[(TT - 1) % 4], 16)
            for b in range(4):
                q.wait_ge(SCRb[b], 16 * 4)
            q.wait_ge(LDC, 16 * 17)
            for kc in range(DK):
                q.dma_start(out=bass.AP(xn_fm, kc * T, [[DK * T, 128], [1, T]]),
                            in_=xn_scr.ap()[0:T, kc * 128:(kc + 1) * 128],
                            transpose=True).then_inc(XNFM, 16)
            q.wait_ge(REV, 1)
            q.dma_start(out=bass.AP(cc1_in, 0, [[64, 1], [1, 32]]), in_=tot_f[:]
                        ).then_inc(CCS, 16)
            q.wait_ge(CCT, 1)
            q.dma_start(out=cagf[:], in_=bass.AP(cc1_out, 0, [[64, 8], [1, 32]])
                        ).then_inc(CCR, 16)
            q.wait_ge(S_CAR, 1)
            q.dma_start(out=bass.AP(cc2_in, 0, [[1, 16], [1, 1]]),
                        in_=Acum[:, T - 1:T]).then_inc(CCS, 16)
            q.dma_start(out=bass.AP(cc2_in, 16, [[1, 16], [1, 1]]),
                        in_=h_loc[:, T - 1:T]).then_inc(CCS, 16)
            for t in range(TT):
                if t >= 4:
                    q.wait_ge(SCTb[t % 4], 64 * (t // 4))
                q.dma_start(out=xnbf[t % 4][:],
                            in_=xn_scr.ap()[t * 128:(t + 1) * 128, :]
                            ).then_inc(SCRb[t % 4], 16)
            q.wait_ge(CCT, 2)
            q.dma_start(out=a_all[:], in_=bass.AP(cc2_out, 0, [[1, 16], [64, 8]]),
                        allow_slow_non_contiguous=True).then_inc(CCS, 16)
            q.dma_start(out=b_all[:], in_=bass.AP(cc2_out, 16, [[1, 16], [64, 8]]),
                        allow_slow_non_contiguous=True).then_inc(CCS, 16)
            for b in range(4):
                q.wait_ge(SCTb[b], 64 * 4)
            for e in range(NE):
                q.wait_ge(FD, 2)
                if e >= 2:
                    q.wait_ge(F_B, e - 1)
                for kc in range(DK):
                    q.dma_start(
                        out=bass.AP(xe2[e % 2], kc * CL, [[DK * CL, 128], [1, CL]]),
                        in_=xe_tbl.ap()[e * CL:(e + 1) * CL, kc * 128:(kc + 1) * 128],
                        transpose=True).then_inc(XELb[e % 2], 16)
            for t in range(TT):
                if t >= 4:
                    q.wait_ge(FCV, t - 3)
                q.dma_start(out=x2[t % 4][:],
                            in_=IN["x_sh"].ap()[t * 128:(t + 1) * 128, :]
                            ).then_inc(FLb[t % 4], 16)
                if t == 0:
                    q.wait_ge(ZSTb[0], 16 * TT)
                    q.wait_ge(ZSTb[1], 16 * TT)
                    q.wait_ge(EOSb[0], 16 * 3 * NE)
                    q.wait_ge(EOSb[1], 16 * 3 * NE)
                q.dma_start(out=z2[t % 4][:],
                            in_=z_scr.ap()[t * 128:(t + 1) * 128, :]
                            ).then_inc(FLb[t % 4], 16)
                if t >= 2:
                    q.wait_ge(FCV, t - 1)
                    q.dma_start(out=y_out.ap()[(t - 2) * 128:(t - 1) * 128, :],
                                in_=y2[t % 2][:]).then_inc(YSTb[t % 2], 16)
            for t in range(TT - 2, TT):
                q.wait_ge(FCV, t + 1)
                q.dma_start(out=y_out.ap()[t * 128:(t + 1) * 128, :],
                            in_=y2[t % 2][:]).then_inc(YSTb[t % 2], 16)
            q.wait_ge(YSTb[0], 16 * 8)
            q.wait_ge(YSTb[1], 16 * 8)
            if DEBUG_TAPS:
                tsrc = {"tp_lg": lg_all, "tp_ea": e_all, "tp_idx": idxf,
                        "tp_wn": wn_all, "tp_wt": wt_all, "tp_lp": lpos_all,
                        "tp_gp": gpos_all, "tp_sd": sdest, "tp_gd": gdest,
                        "tp_cnt": cnt_bf, "tp_cag": cagf, "tp_coff": coff,
                        "tp_tot": tot_f}
                n = 0
                for nm, sbv in tsrc.items():
                    q.dma_start(out=taps[nm].ap(), in_=sbv[:]).then_inc(LDC, 16)
                    n += 1
                q.dma_start(out=taps["tp_hf"].ap(), in_=h_f[:]).then_inc(LDC, 16)
                q.dma_start(out=taps["tp_z16"].ap(), in_=z16[:]).then_inc(LDC, 16)
                q.dma_start(out=taps["tp_hz"].ap(), in_=hz[:]).then_inc(LDC, 16)
                q.wait_ge(LDC, 16 * (17 + n + 3))

        @block.gpsimd
        def _(q):
            q.dma_start(out=wg_sb[:], in_=IN["wgT_t"].ap()).then_inc(WFR, 16)
            q.dma_start(out=w_inT[:], in_=IN["w_inT_t"].ap()).then_inc(WFR, 16)
            q.dma_start(out=wxdt_sb[:], in_=IN["wxdtT_t"].ap()).then_inc(WFR, 16)
            q.dma_start(out=wout_sb[:], in_=IN["w_out16T"].ap()).then_inc(WFR, 16)
            q.wait_ge(XDM, CTN)
            q.wait_ge(XCE, 68)
            q.wait_ge(XCS, CTN)
            q.wait_ge(CVD, CTN)
            q.wait_ge(XCEH, CTN)
            q.wait_ge(S_HFM, DK)
            q.dma_start(out=sw1_sb[:], in_=IN["sw1_t"].ap()).then_inc(SWL, 16)
            q.dma_start(out=sw3_sb[:], in_=IN["sw3_t"].ap()).then_inc(SWL, 16)
            q.wait_ge(CCS, 16)
            q.collective_compute(
                "AllGather", ALU.bypass, replica_groups=[list(range(NCORE))],
                ins=[cc1_in.ap().opt()], outs=[cc1_out.ap().opt()]).then_inc(CCT, 1)
            q.wait_ge(CCS, 48)
            q.collective_compute(
                "AllGather", ALU.bypass, replica_groups=[list(range(NCORE))],
                ins=[cc2_in.ap().opt()], outs=[cc2_out.ap().opt()]).then_inc(CCT, 1)
            for t in range(TT):
                q.wait_ge(SCRb[t % 4], 16 * (4 + t // 4 + 1))
                if t == 0:
                    q.wait_ge(RTBF, 1)
                for k in range(4):
                    q.indirect_dma_start(
                        out=xe_tbl.ap(),
                        out_offset=IndirectOffsetOnAxis(
                            ap=sdest[:, t * 4 + k:t * 4 + k + 1], axis=0),
                        in_=xnbf[t % 4][:], in_offset=None).then_inc(SCTb[t % 4], 16)
            q.wait_ge(SHP, 64)
            q.dma_start(out=sw2_sb[:], in_=IN["sw2_t"].ap()).then_inc(SWL, 16)
            for e in range(NE):
                if e == 0:
                    q.wait_ge(FD, 2)
                if e >= 1:
                    q.wait_ge(F_A, e)
                q.dma_start(out=w13h[0][:], in_=IN["w13_t"].ap()[e, 0]).then_inc(EWA, 16)
                if e >= 1:
                    q.wait_ge(F_B, e)
                q.dma_start(out=w13h[1][:], in_=IN["w13_t"].ap()[e, 1]).then_inc(EWB, 16)
                if e >= 2:
                    q.wait_ge(EO, 6 * (e - 1))
                q.dma_start(out=w2b[e % 2][:], in_=IN["w2_t"].ap()[e]
                            ).then_inc(EW2b[e % 2], 16)
            q.wait_ge(EOSb[0], 16 * 3 * NE)
            q.wait_ge(EOSb[1], 16 * 3 * NE)
            for t in range(TT):
                if t >= 2:
                    q.wait_ge(FCV, t - 1)
                for k in range(4):
                    q.indirect_dma_start(
                        out=gat[k][t % 2][:], out_offset=None,
                        in_=out_tbl.ap(),
                        in_offset=IndirectOffsetOnAxis(
                            ap=gdest[:, t * 4 + k:t * 4 + k + 1], axis=0)
                        ).then_inc(GTHb[t % 2], 16)

        @block.tensor
        def _(q):
            q.wait_ge(S_HALO, 1)
            for kc in range(DK):
                q.wait_ge(S_HFM, kc)
                q.transpose(p_htp[:, 0:3], xnh_bf[0:3, kc * 128:(kc + 1) * 128],
                            c_iden[0:3, 0:3]).then_inc(S_HTP, 1)
            q.wait_ge(XNFM, 16 * DK)
            q.wait_ge(WFR, 64)
            for t in range(TT):
                if t >= 2:
                    q.wait_ge(LGT, t - 1)
                for kc in range(DK):
                    mm = q.matmul(p_gate[t % 2][:],
                                  bass.AP(xn_fm, kc * T + t * 128,
                                          [[DK * T, 128], [1, 128]]),
                                  wg_sb[:, kc * 32:(kc + 1) * 32],
                                  start=(kc == 0), stop=(kc == DK - 1))
                    if kc == DK - 1:
                        mm.then_inc(GTP, 1)
            q.wait_ge(WFR, 64)
            q.wait_ge(LGT, TT)
            q.wait_ge(S_HFM, DK)

            def emit_xz(ct):
                m = 128 if ct < CTN else 16
                if ct < CTN:
                    q.wait_ge(XCEH, ct)
                    for kc in range(DK):
                        mm = q.matmul(p_hl[0:m, :],
                                      w_inT[:, kc * (CH + 16) + ct * 128:
                                            kc * (CH + 16) + ct * 128 + m],
                                      xnh_fm[:, kc * 3:(kc + 1) * 3],
                                      start=(kc == 0), stop=(kc == DK - 1))
                        if kc == DK - 1:
                            mm.then_inc(XZH, 1)
                for n in range(4):
                    ci = ct * 4 + n
                    if ci >= 2:
                        q.wait_ge(XCE, ci - 1)
                    for kc in range(DK):
                        off = kc * (CH + 16) + (ct * 128 if ct < CTN else CH)
                        mm = q.matmul(p_xz[ci % 2][0:m, :],
                                      w_inT[:, off:off + m],
                                      bass.AP(xn_fm, kc * T + n * 512,
                                              [[DK * T, 128], [1, 512]]),
                                      start=(kc == 0), stop=(kc == DK - 1))
                        if kc == DK - 1:
                            mm.then_inc(XZP, 1)

            def emit_xdbl(ct):
                q.wait_ge(XCS, ct + 1)
                for n in range(4):
                    mm = q.matmul(p_xd[n][:, :],
                                  wxdt_sb[:, ct * 80:(ct + 1) * 80],
                                  xc2[ct % 2][:, n * 512:(n + 1) * 512],
                                  start=(ct == 0), stop=(ct == CTN - 1))
                    if n == 3:
                        mm.then_inc(XDM, 1)

            for ct in range(CTN + 1):
                emit_xz(ct)
                if ct >= 2:
                    emit_xdbl(ct - 2)
            emit_xdbl(CTN - 1)
            q.wait_ge(SWL, 32)
            q.wait_ge(XCE, 68)
            q.wait_ge(XCEH, CTN)
            q.wait_ge(XDE, 4)
            for ic in range(8):
                for n in range(4):
                    si = ic * 4 + n
                    if si >= 2:
                        q.wait_ge(SHV, si - 1)
                    for kc in range(DK):
                        mm = q.matmul(p_sh1[si % 2][:],
                                      sw1_sb[:, kc * 1024 + ic * 128:
                                             kc * 1024 + ic * 128 + 128],
                                      bass.AP(xn_fm, kc * T + n * 512,
                                              [[DK * T, 128], [1, 512]]),
                                      start=(kc == 0), stop=(kc == DK - 1))
                        if kc == DK - 1:
                            mm.then_inc(SHP, 1)
                    for kc in range(DK):
                        mm = q.matmul(p_sh3[si % 2][:],
                                      sw3_sb[:, kc * 1024 + ic * 128:
                                             kc * 1024 + ic * 128 + 128],
                                      bass.AP(xn_fm, kc * T + n * 512,
                                              [[DK * T, 128], [1, 512]]),
                                      start=(kc == 0), stop=(kc == DK - 1))
                        if kc == DK - 1:
                            mm.then_inc(SHP, 1)
            q.wait_ge(SHV, 32)
            for t in range(TT):
                q.wait_ge(RT3, t + 1)
                mm = q.matmul(p_tot[0:1, :], c_onesc[:], cnt_bf[:, t * 32:(t + 1) * 32],
                              start=(t == 0), stop=(t == TT - 1))
                if t == TT - 1:
                    mm.then_inc(RMM, 1)
            q.wait_ge(CAGB, 1)
            q.matmul(p_coff[:], c_strict[:], cag_hi[:], start=True, stop=False)
            q.matmul(p_coff[:], c_strict[:], cag_lo[:], start=False, stop=True
                     ).then_inc(RMM, 1)
            for t in range(TT):
                if t >= 2:
                    q.wait_ge(RTB, t - 1)
                for tp in range(t):
                    q.matmul(p_pfx[t % 2][:], c_onesf[:],
                             cnt_bf[:, tp * 32:(tp + 1) * 32],
                             start=(tp == 0), stop=False)
                q.matmul(p_pfx[t % 2][:], c_tincl[:], cnt_bf[:, t * 32:(t + 1) * 32],
                         start=(t == 0), stop=True).then_inc(PFX, 1)
            q.wait_ge(SWL, 48)
            for t in range(TT):
                for n in range(2):
                    zi = t * 2 + n
                    if zi >= 2:
                        q.wait_ge(ZSE, zi - 1)
                    for ic in range(8):
                        mm = q.matmul(p_zs[zi % 2][:],
                                      bass.AP(hsh, ic * T + t * 128,
                                              [[8 * T, 128], [1, 128]]),
                                      sw2_sb[:, ic * 1024 + n * 512:
                                             ic * 1024 + n * 512 + 512],
                                      start=(ic == 0), stop=(ic == 7))
                        if ic == 7:
                            mm.then_inc(ZSP, 1)
            q.wait_ge(RTB, TT)
            q.wait_ge(REV, 2)
            q.wait_ge(ZSE, 2 * TT)
            for e in range(NE):
                q.wait_ge(XELb[e % 2], 16 * DK * (e // 2 + 1))
                for ic in range(8):
                    hw = ic // 4
                    icw = ic % 4
                    q.wait_ge(EWA if hw == 0 else EWB, 16 * (e + 1))
                    gi = e * 8 + ic
                    if gi >= 2:
                        q.wait_ge(EHV, gi - 1)
                    for kc in range(DK):
                        mm = q.matmul(p_eh1[ic % 2][:],
                                      w13h[hw][:, kc * 1024 + icw * 128:
                                              kc * 1024 + icw * 128 + 128],
                                      bass.AP(xe2[e % 2], kc * CL,
                                              [[DK * CL, 128], [1, CL]]),
                                      start=(kc == 0), stop=(kc == DK - 1))
                        if kc == DK - 1:
                            mm.then_inc(EH, 1)
                    for kc in range(DK):
                        mm = q.matmul(p_eh3[ic % 2][:],
                                      w13h[hw][:, kc * 1024 + 512 + icw * 128:
                                              kc * 1024 + 512 + icw * 128 + 128],
                                      bass.AP(xe2[e % 2], kc * CL,
                                              [[DK * CL, 128], [1, CL]]),
                                      start=(kc == 0), stop=(kc == DK - 1))
                        if kc == DK - 1:
                            if ic == 3:
                                mm.then_inc(F_A, 1)
                            elif ic == 7:
                                mm.then_inc(F_B, 1)
                            else:
                                mm.then_inc(EH, 1)
                q.wait_ge(EHV, e * 8 + 8)
                q.wait_ge(EW2b[e % 2], 16 * (e // 2 + 1))
                for st in range(3):
                    for n in range(2):
                        oi = e * 6 + st * 2 + n
                        if oi >= 2:
                            q.wait_ge(EOE, oi - 1)
                        for ic in range(8):
                            mm = q.matmul(p_eo[oi % 2][:],
                                          bass.AP(h_e[e % 2], ic * CL + st * 128,
                                                  [[8 * CL, 128], [1, 128]]),
                                          w2b[e % 2][:, ic * 1024 + n * 512:
                                                     ic * 1024 + n * 512 + 512],
                                          start=(ic == 0), stop=(ic == 7))
                            if ic == 7:
                                mm.then_inc(EO, 1)
            q.wait_ge(S_HZ, 1)
            q.wait_ge(WFR, 64)
            q.wait_ge(EOE, 6 * NE)
            q.wait_ge(EHV, 8 * NE)
            for t in range(TT):
                if t >= 2:
                    q.wait_ge(FCV, t - 1)
                for n in range(2):
                    q.matmul(p_mo[t % 2][n][:], hz[:, t * 128:(t + 1) * 128],
                             wout_sb[:, n * 512:(n + 1) * 512],
                             start=True, stop=True).then_inc(FPM, 1)

        @block.scalar
        def _(q):
            a = Ch(q, ACH)
            q.wait_ge(LDC, 16 * 17)
            q.wait_ge(S_VAR, 1)
            a.do(q.activation(sqv[0:3, :], vtmp[0:3, :], AF.Sqrt))
            q.sem_inc(S_SQ, 1)
            for t in range(TT):
                q.wait_ge(S_VAR, t + 2)
                a.do(q.activation(sqv[:], vtmp[:], AF.Sqrt))
                q.sem_inc(S_SQ, 1)
            for kc in range(DK):
                q.wait_ge(S_HTP, kc + 1)
                a.do(q.copy(xnh_fm[:, kc * 3:(kc + 1) * 3], p_htp[:, 0:3]))
                q.sem_inc(S_HFM, 1)
            for t in range(TT):
                q.wait_ge(GTP, t + 1)
                a.do(q.copy(lg_all[:, t * 32:(t + 1) * 32], p_gate[t % 2][:]))
                q.sem_inc(LGT, 1)
            for ct in range(CTN + 1):
                m = 128 if ct < CTN else 16
                if ct < CTN:
                    q.wait_ge(XZH, ct + 1)
                    if ct >= 2:
                        q.wait_ge(CVD, ct - 1)
                    a.do(q.activation(xcp2[ct % 2][0:m, 0:3], p_hl[0:m, :],
                                      AF.Identity, bias=halo_b[:, ct:ct + 1]))
                    q.sem_inc(XCEH, 1)
                for n in range(4):
                    ci = ct * 4 + n
                    q.wait_ge(XZP, ci + 1)
                    if ct < CTN:
                        a.do(q.activation(
                            xcp2[ct % 2][0:m, 3 + n * 512:3 + (n + 1) * 512],
                            p_xz[ci % 2][0:m, :], AF.Identity,
                            bias=c_b_in_xc[:, ct:ct + 1]))
                    else:
                        a.do(q.activation(z16[0:16, n * 512:(n + 1) * 512],
                                          p_xz[ci % 2][0:16, :], AF.Identity,
                                          bias=c_b_in_z[:]))
                    q.sem_inc(XCE, 1)
                if ct < CTN:
                    q.wait_ge(CVD, ct + 1)
                    if ct >= 2:
                        q.wait_ge(XDM, ct - 1)
                    a.do(q.activation(xc2[ct % 2][:], acc2[ct % 2][:], AF.Silu,
                                      bias=c_b_conv[:, ct:ct + 1]))
                    q.sem_inc(XCS, 1)
                    if ct == 0:
                        a.do(q.activation(xc16[:], acc2[0][0:16, :], AF.Silu,
                                          bias=c_b_conv[0:16, 0:1]))
            q.wait_ge(XDM, CTN)
            for n in range(4):
                sl = slice(n * 512, (n + 1) * 512)
                a.do(q.activation(absA[:, sl], p_xd[n][0:16, :], AF.Abs))
                a.do(q.activation(sp_r[:, sl], p_xd[n][32:48, :], AF.Relu,
                                  bias=c_b_dt[:]))
                a.do(q.activation(sp_t[:, sl], p_xd[n][32:48, :], AF.Abs,
                                  bias=c_b_dt[:]))
                a.do(q.activation(sp_t[:, sl], sp_t[:, sl], AF.Exp,
                                  scale=cm1[0:16, :]))
                a.do(q.activation(sp_t[:, sl], sp_t[:, sl], AF.Ln, bias=c1[0:16, :]))
                a.do(q.activation(xdblB[:, sl], p_xd[n][64:80, :], AF.Identity))
                q.sem_inc(XDE, 1)
            q.wait_ge(S_A1, 1)
            a.do(q.activation(A_sb[:], A_sb[:], AF.Exp, scale=cm1[0:16, :]))
            q.sem_inc(S_A2, 1)
            q.wait_ge(S_EX1, 1)
            a.do(q.activation(e_all[:], e_all[:], AF.Exp))
            q.sem_inc(S_EX2, 1)
            for ic in range(8):
                for n in range(4):
                    si = ic * 4 + n
                    q.wait_ge(SHP, 2 * si + 1)
                    if si >= 2:
                        q.wait_ge(SHV, si - 1)
                    a.do(q.activation(stmp_s[si % 2][:], p_sh1[si % 2][:], AF.Silu))
                    q.sem_inc(SHS, 1)
            q.wait_ge(RMM, 1)
            a.do(q.copy(tot_f[:], p_tot[0:1, :]))
            q.sem_inc(REV, 1)
            q.wait_ge(RMM, 2)
            a.do(q.copy(coff[:], p_coff[:]))
            q.sem_inc(REV, 1)
            for t in range(TT):
                for n in range(2):
                    zi = t * 2 + n
                    q.wait_ge(ZSP, zi + 1)
                    if zi >= 2:
                        q.wait_ge(ZSTb[zi % 2], 16 * (zi // 2))
                    a.do(q.activation(zt2[zi % 2][:], p_zs[zi % 2][:], AF.Identity))
                    q.sem_inc(ZSE, 1)
                    q.dma_start(out=z_scr.ap()[t * 128:(t + 1) * 128,
                                               n * 512:(n + 1) * 512],
                                in_=zt2[zi % 2][:]).then_inc(ZSTb[zi % 2], 16)
            q.wait_ge(ZSTb[0], 16 * TT)
            q.wait_ge(ZSTb[1], 16 * TT)
            q.sem_inc(FD, 1)
            for e in range(NE):
                for ic in range(8):
                    gi = e * 8 + ic
                    q.wait_ge(EH, 14 * e + EH_W1[ic])
                    if gi >= 2:
                        q.wait_ge(EHV, gi - 1)
                    a.do(q.activation(stmp_e[ic % 2][:], p_eh1[ic % 2][:], AF.Silu))
                    q.sem_inc(EHSS, 1)
                for st in range(3):
                    for n in range(2):
                        oi = e * 6 + st * 2 + n
                        q.wait_ge(EO, oi + 1)
                        if oi >= 2:
                            q.wait_ge(EOSb[oi % 2], 16 * (oi // 2))
                        a.do(q.activation(out_e[oi % 2][:], p_eo[oi % 2][:],
                                          AF.Identity))
                        q.sem_inc(EOE, 1)
                        q.dma_start(
                            out=out_tbl.ap()[e * CL + st * 128:e * CL + (st + 1) * 128,
                                             n * 512:(n + 1) * 512],
                            in_=out_e[oi % 2][:]).then_inc(EOSb[oi % 2], 16)

        @block.vector
        def _(q):
            v = Ch(q, VCH)
            v.do(q.memset(zero_bf[:], 0.0))
            q.sem_inc(ZT, 1)
            v.do(q.memset(cm1[:], -1.0))
            v.do(q.memset(c1[:], 1.0))
            q.wait_ge(LDC, 16 * 17)
            v.do(q.tensor_scalar(halo_b[:], c_b_in_xc[:], c_hflag[:], None, ALU.mult))

            def ln_a(xa, P):
                v.do(q.bn_stats(bnst[0:P, 0:6], xa[0:P, 0:512]))
                v.do(q.bn_stats(bnst[0:P, 6:12], xa[0:P, 512:1024]))
                v.do(q.bn_aggr(mv[0:P, :], bnst[0:P, :]))
                v.do(q.tensor_scalar(vtmp[0:P, :], mv[0:P, 1:2], 1e-5, None, ALU.add))

            def ln_b(xa, xnout, P, sq_sem, sq_val):
                q.wait_ge(sq_sem, sq_val)
                v.do(q.reciprocal(rstd[0:P, :], sqv[0:P, :]))
                v.do(q.tensor_scalar(xnout[0:P, :], xa[0:P, :], mv[0:P, 0:1],
                                     rstd[0:P, :], ALU.subtract, ALU.mult))

            ln_a(xh, 3)
            q.sem_inc(S_VAR, 1)
            ln_b(xh, xnh_bf, 3, S_SQ, 1)
            q.sem_inc(S_HALO, 1)
            for t in range(TT):
                q.wait_ge(LDXb[t % 4], 16 * (t // 4 + 1))
                if t >= 4:
                    q.wait_ge(SCRb[t % 4], 16 * (t // 4))
                ln_a(x2[t % 4], 128)
                q.sem_inc(S_VAR, 1)
                ln_b(x2[t % 4], xnbf[t % 4], 128, S_SQ, t + 2)
                q.sem_inc(XNT, 1)
            for ct in range(CTN):
                q.wait_ge(XCE, 4 * (ct + 1))
                q.wait_ge(XCEH, ct + 1)
                if ct >= 2:
                    q.wait_ge(XCS, ct - 1)
                xp = xcp2[ct % 2]
                ac = acc2[ct % 2]
                v.do(q.tensor_scalar(ac[:], xp[:, 0:T],
                                     c_w_conv[:, ct * 4:ct * 4 + 1], None, ALU.mult))
                for j in range(1, 4):
                    v.do(q.scalar_tensor_tensor(
                        ac[:], xp[:, j:j + T],
                        c_w_conv[:, ct * 4 + j:ct * 4 + j + 1],
                        ac[:], ALU.mult, ALU.add))
                q.sem_inc(CVD, 1)
            q.wait_ge(XDE, 4)
            v.do(q.tensor_add(delta[:], sp_r[:], sp_t[:]))
            v.do(q.tensor_mul(A_sb[:], absA[:], delta[:]))
            q.sem_inc(S_A1, 1)
            q.wait_ge(S_A2, 1)
            v.do(q.tensor_mul(Bx_sb[:], xdblB[:], delta[:]))
            v.do(q.tensor_mul(Bx_sb[:], Bx_sb[:], xc16[:]))
            v.do(q.tensor_tensor_scan(h_loc[:], A_sb[:], Bx_sb[:], 0.0,
                                      ALU.mult, ALU.add))
            v.do(q.memset(zeros16[:], 0.0))
            v.do(q.tensor_tensor_scan(Acum[:], A_sb[:], zeros16[:], 1.0,
                                      ALU.mult, ALU.add))
            q.sem_inc(S_CAR, 1)
            q.wait_ge(LGT, TT)
            for t in range(TT):
                v.do(q.reduce_max(nmx[:, t:t + 1], lg_all[:, t * 32:(t + 1) * 32],
                                  axis=AX, negate=True))
            for t in range(TT):
                v.do(q.tensor_scalar(e_all[:, t * 32:(t + 1) * 32],
                                     lg_all[:, t * 32:(t + 1) * 32],
                                     nmx[:, t:t + 1], None, ALU.add))
            q.sem_inc(S_EX1, 1)
            q.wait_ge(S_EX2, 1)
            for t in range(TT):
                c32 = slice(t * 32, (t + 1) * 32)
                v.do(q.reduce_sum(rt1[:], e_all[:, c32], axis=AX))
                v.do(q.reciprocal(rt1[:], rt1[:]))
                v.do(q.tensor_scalar(e_all[:, c32], e_all[:, c32], rt1[:],
                                     None, ALU.mult))
                v.do(q.tensor_tensor(gsc[:],
                                     bass.AP(e_all, t * 32 + 0, [[TT * 32, 128], [4, 8]]),
                                     bass.AP(e_all, t * 32 + 1, [[TT * 32, 128], [4, 8]]),
                                     ALU.max))
                v.do(q.tensor_tensor(g8[:],
                                     bass.AP(e_all, t * 32 + 2, [[TT * 32, 128], [4, 8]]),
                                     bass.AP(e_all, t * 32 + 3, [[TT * 32, 128], [4, 8]]),
                                     ALU.max))
                v.do(q.tensor_tensor(gsc[:], gsc[:], g8[:], ALU.max))
                v.do(q.max(out=g8[:], in_=gsc[:]))
                v.do(q.tensor_scalar(kgB[:], gsc[:], g8[:, 3:4], BIG, ALU.is_ge,
                                     ALU.mult))
                v.do(q.scalar_tensor_tensor(
                    msk[:].rearrange("p (a o) -> p a o", o=4),
                    kgB[:].rearrange("p (a o) -> p a o", o=1).to_broadcast([128, 8, 4]),
                    -BIG,
                    e_all[:, c32].rearrange("p (a o) -> p a o", o=4),
                    ALU.add, ALU.add))
                v.do(q.max(out=mx4[:], in_=msk[:]))
                v.do(q.max_index(out=mxi[:], in_max=mx4[:], in_values=msk[:]))
                v.do(q.tensor_copy(idxf[:, t * 4:t * 4 + 4], mxi[:, 0:4]))
                v.do(q.reduce_sum(rt1[:], mx4[:, 0:4], axis=AX))
                v.do(q.tensor_scalar(rt1[:], rt1[:], 1e-9, None, ALU.add))
                v.do(q.reciprocal(rt1[:], rt1[:]))
                v.do(q.tensor_scalar(wn_all[:, t * 4:t * 4 + 4], mx4[:, 0:4],
                                     rt1[:], None, ALU.mult))
                v.do(q.tensor_scalar(oh_t[:], c_iota[:], idxf[:, t * 4:t * 4 + 1],
                                     None, ALU.is_equal))
                for k in range(1, 4):
                    v.do(q.tensor_scalar(rtmp[:], c_iota[:],
                                         idxf[:, t * 4 + k:t * 4 + k + 1],
                                         None, ALU.is_equal))
                    v.do(q.tensor_add(oh_t[:], oh_t[:], rtmp[:]))
                v.do(q.tensor_copy(cnt_bf[:, t * 32:(t + 1) * 32], oh_t[:]))
                q.sem_inc(RT3, 1)
            for ic in range(8):
                for n in range(4):
                    si = ic * 4 + n
                    q.wait_ge(SHS, si + 1)
                    q.wait_ge(SHP, 2 * si + 2)
                    v.do(q.tensor_mul(
                        bass.AP(hsh, ic * T + n * 512, [[8 * T, 128], [1, 512]]),
                        stmp_s[si % 2][:], p_sh3[si % 2][:]))
                    q.sem_inc(SHV, 1)
            q.sem_inc(FD, 1)
            q.wait_ge(CCR, 16)
            v.do(q.tensor_copy(cag_hi[:], cagf[:]))
            v.do(q.tensor_copy(cag_hif[:], cag_hi[:]))
            v.do(q.tensor_sub(tmpa[0:8, 0:32], cagf[:], cag_hif[:]))
            v.do(q.tensor_copy(cag_lo[:], tmpa[0:8, 0:32]))
            q.sem_inc(CAGB, 1)
            q.wait_ge(REV, 2)
            for t in range(TT):
                q.wait_ge(PFX, t + 1)
                v.do(q.tensor_copy(lposg[:], p_pfx[t % 2][:]))
                q.sem_inc(RTB, 1)
                v.do(q.tensor_sub(lposg[:], lposg[:], cnt_bf[:, t * 32:(t + 1) * 32]))
                v.do(q.tensor_add(gposg[:], lposg[:], coff[:]))
                for k in range(4):
                    c0 = t * 4 + k
                    v.do(q.tensor_scalar(oh_t[:], c_iota[:], idxf[:, c0:c0 + 1],
                                         None, ALU.is_equal))
                    v.do(q.tensor_mul(rtmp[:], oh_t[:], lposg[:]))
                    v.do(q.reduce_sum(lpos_all[:, c0:c0 + 1], rtmp[:], axis=AX))
                    v.do(q.tensor_mul(rtmp[:], oh_t[:], gposg[:]))
                    v.do(q.reduce_sum(gpos_all[:, c0:c0 + 1], rtmp[:], axis=AX))
            v.do(q.tensor_scalar(tmpa[:], gpos_all[:], float(CAP), None, ALU.is_lt))
            v.do(q.tensor_scalar(tmpb[:], lpos_all[:], float(CL), None, ALU.is_lt))
            v.do(q.tensor_mul(tmpa[:], tmpa[:], tmpb[:]))
            v.do(q.tensor_mul(wt_all[:], wn_all[:], tmpa[:]))
            v.do(q.tensor_scalar(tmpb[:], idxf[:], float(CL), None, ALU.mult))
            v.do(q.tensor_add(tmpb[:], tmpb[:], lpos_all[:]))
            v.do(q.tensor_mul(tmpb[:], tmpb[:], tmpa[:]))
            v.do(q.tensor_scalar(gpos_all[:], tmpa[:], float(-SLOT_TRASH),
                                 float(SLOT_TRASH), ALU.mult, ALU.add))
            v.do(q.tensor_add(gpos_all[:], gpos_all[:], tmpb[:]))
            v.do(q.tensor_copy(sdest[:], gpos_all[:]))
            v.do(q.tensor_scalar(gpos_all[:], tmpa[:], float(-SLOT_ZERO),
                                 float(SLOT_ZERO), ALU.mult, ALU.add))
            v.do(q.tensor_add(gpos_all[:], gpos_all[:], tmpb[:]))
            v.do(q.tensor_copy(gdest[:], gpos_all[:]))
            q.sem_inc(RTBF, 1)
            q.wait_ge(CCS, 80)
            v.do(q.tensor_mul(atl[:], a_all[:], c_m16[:]))
            v.do(q.tensor_add(atl[:], atl[:], c_im16[:]))
            v.do(q.memset(sfx[:], 1.0))
            v.do(q.tensor_copy(sfx[:, 0:8], atl[:]))
            for s_ in (1, 2, 4):
                v.do(q.tensor_mul(sfx[:, 0:8], sfx[:, 0:8], sfx[:, s_:s_ + 8]))
            v.do(q.tensor_mul(bm_[:], b_all[:], c_m16[:]))
            v.do(q.tensor_mul(bm_[:], bm_[:], sfx[:, 1:9]))
            v.do(q.reduce_sum(h_in[:], bm_[:], axis=AX))
            v.do(q.scalar_tensor_tensor(h_f[:], Acum[:], h_in[:], h_loc[:],
                                        ALU.mult, ALU.add))
            q.wait_ge(XCE, 68)
            v.do(q.tensor_mul(hz[:], h_f[:], z16[:]))
            q.sem_inc(S_HZ, 1)
            for e in range(NE):
                for ic in range(8):
                    gi = e * 8 + ic
                    q.wait_ge(EHSS, gi + 1)
                    if ic == 3:
                        q.wait_ge(F_A, e + 1)
                    elif ic == 7:
                        q.wait_ge(F_B, e + 1)
                    else:
                        q.wait_ge(EH, 14 * e + EH_W3[ic])
                    if e >= 2 and ic == 0:
                        q.wait_ge(EO, 6 * (e - 1))
                    v.do(q.tensor_mul(
                        bass.AP(h_e[e % 2], ic * CL, [[8 * CL, 128], [1, CL]]),
                        stmp_e[ic % 2][:], p_eh3[ic % 2][:]))
                    q.sem_inc(EHV, 1)
            for t in range(TT):
                q.wait_ge(FLb[t % 4], 32 * (t // 4 + 1))
                q.wait_ge(FPM, 2 * (t + 1))
                q.wait_ge(GTHb[t % 2], 64 * (t // 2 + 1))
                if t >= 2:
                    q.wait_ge(YSTb[t % 2], 16 * (t // 2))
                yb = y2[t % 2]
                v.do(q.tensor_add(yb[:, 0:512], x2[t % 4][:, 0:512], p_mo[t % 2][0][:]))
                v.do(q.tensor_add(yb[:, 512:1024], x2[t % 4][:, 512:1024],
                                  p_mo[t % 2][1][:]))
                v.do(q.tensor_add(yb[:], yb[:], z2[t % 4][:]))
                for k in range(4):
                    v.do(q.scalar_tensor_tensor(yb[:], gat[k][t % 2][:],
                                                wt_all[:, t * 4 + k:t * 4 + k + 1],
                                                yb[:], ALU.mult, ALU.add))
                q.sem_inc(FCV, 1)

    return nc


def kernel(**inputs):
    sh, cores = host_prep(inputs)
    nc = build_nc()
    in_maps = []
    for c in range(NCORE):
        m = dict(sh)
        m.update(cores[c])
        in_maps.append(m)
    res = run_bass_kernel_spmd(nc, in_maps, core_ids=list(range(NCORE)))
    out = np.concatenate([np.asarray(res.results[c]["y_out"]) for c in range(NCORE)],
                         axis=0)
    return out.reshape(1, 16384, D).astype(np.float32)
